# version 01
# baseline (speedup 1.0000x reference)
import numpy as np
from contextlib import ExitStack

import concourse.bass as bass
import concourse.mybir as mybir
from concourse.bass_utils import run_bass_kernel_spmd

F32 = mybir.dt.float32
BF16 = mybir.dt.bfloat16
AF = mybir.ActivationFunctionType
ALU = mybir.AluOpType
AX = mybir.AxisListType

D = 2048
FF = 1024
NE = 8
S = 4096
OWN = 2048
PT = 1024
NT = PT // 128
NK = D // 128
NJ = FF // 128
NSC = S // 128
EPS = 1e-5
A_PASSES = 4
C_PASSES = 2
NEX = NE
B_SUB = 4
ENGS = ("sync", "scalar", "vector", "tensor", "gpsimd")


class Chans:
    def __init__(self, sems):
        self.sems = sems
        self.map = {}
        self.counts = {}
        self.nused = 0

    def _get(self, name):
        if name not in self.map:
            if self.nused >= len(self.sems):
                raise RuntimeError("out of semaphores")
            self.map[name] = self.sems[self.nused]
            self.nused += 1
            self.counts[name] = 0
        return self.map[name]

    def produce(self, name, inst, k=1):
        sem = self._get(name)
        self.counts[name] += k
        if inst is not None:
            inst.then_inc(sem, k)

    def wait(self, eng, name, val=None):
        sem = self._get(name)
        v = self.counts[name] if val is None else val
        if eng is not None and v > 0:
            eng.wait_ge(sem, v)

    def count(self, name):
        self._get(name)
        return self.counts[name]


def build(phases=("A", "B", "C"), dbg=()):
    nc = bass.Bass("TRN2", target_bir_lowering=False, debug=False,
                   detect_race_conditions=False)

    x_in = nc.dram_tensor("x_in", [S, D], F32, kind="ExternalInput")
    wq_in = nc.dram_tensor("wq_in", [D, D], F32, kind="ExternalInput")
    wk_in = nc.dram_tensor("wk_in", [D, D], F32, kind="ExternalInput")
    wv_in = nc.dram_tensor("wv_in", [D, D], F32, kind="ExternalInput")
    wo_in = nc.dram_tensor("wo_in", [D, D], F32, kind="ExternalInput")
    rT_in = nc.dram_tensor("rT_in", [D, NE], F32, kind="ExternalInput")
    w1_in = nc.dram_tensor("w1_in", [NE, D, FF], F32, kind="ExternalInput")
    w2_in = nc.dram_tensor("w2_in", [NE, D, FF], F32, kind="ExternalInput")
    w3_in = nc.dram_tensor("w3_in", [NE, FF, D], F32, kind="ExternalInput")
    id_in = nc.dram_tensor("id_in", [128, 128], F32, kind="ExternalInput")
    y_out = nc.dram_tensor("y", [OWN, D], F32, kind="ExternalOutput")
    dbg_outs = {}
    for name, shp, dt in (("lnb", [NSC, NK, 128, 128], BF16),
                          ("st", [8, 128, 1], F32), ("acc0", [128, D], F32),
                          ("lnT", [NK, 128, PT], BF16), ("sh", [NJ, 128, PT], BF16),
                          ("gate", [NT, 128, NE], F32), ("lg", [8, PT], F32),
                          ("X1", [S, D], F32), ("X2", [OWN, D], F32),
                          ("qd", [S, D], BF16), ("kd", [S, D], BF16),
                          ("vt", [OWN, D], BF16), ("Pd", [NK, 128, D], BF16),
                          ("Wt", [NK, 128, D], BF16)):
        if name in dbg:
            dbg_outs[name] = nc.dram_tensor("dbg_" + name, shp, dt,
                                            kind="ExternalOutput")

    X1_d = nc.dram_tensor("X1_d", [S, D], F32)
    X2_d = nc.dram_tensor("X2_d", [OWN, D], F32)
    lnb_d = nc.dram_tensor("lnb_d", [NSC, NK, 128, 128], BF16)
    q_d = nc.dram_tensor("q_d", [S, D], BF16)
    k_d = nc.dram_tensor("k_d", [S, D], BF16)
    vt_d = nc.dram_tensor("vt_d", [OWN, D], BF16)
    P_d = nc.dram_tensor("P_d", [NK, 128, D], BF16)
    Wt_d = nc.dram_tensor("Wt_d", [NK, 128, D], BF16)

    SB_TOP = nc.sbuf_top
    ARENA_BYTES = ((SB_TOP - nc.sbuf_base - 8192) // 32) * 32
    arena_t = nc.alloc_sbuf_tensor("big_arena", [128, ARENA_BYTES],
                                   mybir.dt.uint8)
    ARENA_BASE = nc.lookup_mloc(arena_t).addr

    class Arena:
        def __init__(self, start):
            self.off = (start + 31) & ~31

        def al(self, name, shape, dtype):
            nbytes = int(np.prod(shape[1:])) * mybir.dt.size(dtype)
            self.off = (self.off + 31) & ~31
            t = nc.alloc_sbuf_tensor_at(name, shape, dtype, offset=self.off)
            self.off += nbytes
            assert self.off <= SB_TOP, f"SBUF overflow at {name}: {self.off}>{SB_TOP}"
            return t

    G = Arena(ARENA_BASE)
    identb = G.al("identb", [128, 128], BF16)
    identf = G.al("identf", [128, 128], F32)
    routerTb = G.al("routerTb", [128, NK * NE], BF16)
    st_sum = G.al("st_sum", [128, 1], F32)
    st_sq1 = G.al("st_sq1", [128, 1], F32)
    st_sq2 = G.al("st_sq2", [128, 1], F32)
    st_mean = G.al("st_mean", [128, 1], F32)
    st_msq = G.al("st_msq", [128, 1], F32)
    st_var = G.al("st_var", [128, 1], F32)
    st_rstd = G.al("st_rstd", [128, 1], F32)
    st_lnv = G.al("st_lnv", [128, 1], F32)
    eps_t = G.al("eps_t", [128, 1], F32)
    sm_m = G.al("sm_m", [128, 1], F32)
    sm_b = G.al("sm_b", [128, 1], F32)
    sm_s = G.al("sm_s", [128, 1], F32)
    sm_r = G.al("sm_r", [128, 1], F32)
    g_max8 = G.al("g_max8", [128, 8], F32)
    g_d = G.al("g_d", [128, 1], F32)
    g_s1 = G.al("g_s1", [128, 1], F32)
    g_s2 = G.al("g_s2", [128, 1], F32)
    g_i1 = G.al("g_i1", [128, NE], F32)
    g_i2 = G.al("g_i2", [128, NE], F32)
    g_t1 = G.al("g_t1", [128, NE], F32)
    g_lgtok = G.al("g_lgtok", [128, NE], F32)
    gatetok = [G.al(f"gatetok{t}", [128, NE], F32) for t in range(NT)]
    PHB = G.off

    FA = Arena(PHB)
    acc = [FA.al(f"acc{t}", [128, D], F32) for t in range(NT)]
    lntok = FA.al("lntok", [128, D], BF16)
    lg_sb = FA.al("lg_sb", [8, PT], F32)
    LNT_OFF = FA.off
    lnT = [FA.al(f"lnT{k}", [128, PT], BF16) for k in range(NK)]
    WS_OFF = FA.off
    wslot = [FA.al(f"wslot{i}", [128, 16 * FF], BF16) for i in range(2)]
    s_h = [FA.al(f"s_h{j}", [128, PT], BF16) for j in range(NJ)]
    FFN_TOP = FA.off
    assert FFN_TOP <= SB_TOP

    B1 = Arena(LNT_OFF)
    lnblk_st = [B1.al(f"lnbst{i}", [128, NK * 128], BF16) for i in range(2)]
    lnblk = [B1.al(f"lnblk{i}", [128, NK * 128], BF16) for i in range(2)]
    b1st = [B1.al(f"b1st{i}", [128, D], BF16) for i in range(2)]
    assert B1.off <= WS_OFF
    wsp = nc.alloc_sbuf_tensor_at("wsp", [128, NK * D], BF16, offset=WS_OFF)

    B2 = Arena(PHB)
    kring = [B2.al(f"kring{i}", [128, D], BF16) for i in range(4)]
    qep = [B2.al(f"qep{i}", [128, NSC * 256], BF16) for i in range(2)]
    pstage = [B2.al(f"pst{i}", [128, D], BF16) for i in range(2)]
    assert B2.off <= SB_TOP

    B3 = Arena(PHB)
    wob = B3.al("wob", [128, NK * D], BF16)
    pblk = [B3.al(f"pblk{i}", [128, NK * 128], BF16) for i in range(2)]
    wtstage = [B3.al(f"wtst{i}", [128, D], BF16) for i in range(2)]
    assert B3.off <= SB_TOP

    B4 = Arena(PHB)
    vTb = B4.al("vTb", [128, NK * D], BF16)
    wtb = B4.al("wtb", [128, NK * D], BF16)
    vtr = [B4.al(f"vtr{i}", [128, D], BF16) for i in range(2)]
    x1r = [B4.al(f"x1r{i}", [128, D], F32) for i in range(2)]
    x2st = [B4.al(f"x2st{i}", [128, D], F32) for i in range(2)]
    assert B4.off <= SB_TOP

    pBig1 = nc.alloc_psum_tensor("pBig1", [128, 2048], F32)
    pBig2 = nc.alloc_psum_tensor("pBig2", [128, 2048], F32)
    pB1b = pBig1.bitcast(BF16)
    RV = [pBig1[:, 0:1024], pBig1[:, 1024:2048],
          pBig2[:, 0:1024], pBig2[:, 1024:2048]]
    PBIG = [pBig1, pBig2]
    TSL = [pB1b[:, 0:128], pB1b[:, 2048:2048 + 128]]
    GSL = [pBig1[:, 0:8], pBig1[:, 1024:1024 + 8]]

    es = ExitStack()
    sems = [es.enter_context(nc.semaphore(f"c{i}")) for i in range(98)]

    def emit(en, eng, C: Chans):
        he = {n: (eng if en == n else None) for n in ENGS}
        barctr = [0]

        def dma(q, dst, src, ch):
            if en == q:
                C.produce(ch, eng.dma_start(out=dst, in_=src), 16)
            else:
                C.produce(ch, None, 16)

        def claim(regs):
            for r in regs:
                C.wait(he["tensor"], f"Rf{r}")

        def group_done(regs, inst, extra=()):
            if inst is not None:
                C.produce(f"Rg{regs[0]}", inst)
                for x in extra:
                    C.produce(x, inst)
            else:
                C.produce(f"Rg{regs[0]}", None)
                for x in extra:
                    C.produce(x, None)

        def free_regs(regs, inst, extra=()):
            if inst is not None:
                for r in regs:
                    C.produce(f"Rf{r}", inst)
                for x in extra:
                    C.produce(x, inst)
            else:
                for r in regs:
                    C.produce(f"Rf{r}", None)
                for x in extra:
                    C.produce(x, None)

        def barrier(drains=()):
            for ch in drains:
                C.wait(eng, ch)
            sem = C._get("bar")
            eng.sem_inc(sem, 1)
            barctr[0] += 1
            eng.wait_ge(sem, 5 * barctr[0])

        wfree_req = {0: [], 1: []}

        dma("gpsimd", identb[:, :], id_in[:, :], "constg")
        dma("gpsimd",
            routerTb.ap().rearrange("p (k e) -> p k e", e=NE),
            rT_in.ap().rearrange("(k r) e -> r k e", r=128), "constg")
        dma("sync", identf[:, :], id_in[:, :], "consts")
        for n in ("tensor", "vector", "scalar"):
            C.wait(he[n], "constg")
            C.wait(he[n], "consts")
        if en == "vector":
            C.produce("const1", eng.memset(eps_t[:, :], EPS), 1)
        else:
            C.produce("const1", None, 1)
        C.wait(he["scalar"], "const1")

        def ln_pass(xsrc, row0, mode, guard_ch, blk0=0, lnT_dst=None):
            router = mode == "chunks"
            for t in range(NT):
                C.wait(he["sync"], guard_ch)
                dma("sync", acc[t][:, :],
                    xsrc[row0 + t * 128: row0 + (t + 1) * 128, :], f"xin{t}")
                C.wait(he["scalar"], f"xin{t}")
                C.wait(he["scalar"], "var")
                C.wait(he["scalar"], "lnTc")
                if en == "scalar":
                    i2 = eng.activation(lntok[:, 0:1024], acc[t][:, 0:1024],
                                        AF.Square, accum_out=st_sq1[:, :])
                    C.produce("sch", i2)
                    C.wait(eng, "sch")
                    i2 = eng.activation(lntok[:, 1024:2048], acc[t][:, 1024:2048],
                                        AF.Square, accum_out=st_sq2[:, :])
                    C.produce("sq", i2)
                else:
                    C.produce("sch", None)
                    C.produce("sq", None)
                C.wait(he["vector"], f"xin{t}")
                C.wait(he["vector"], "sq")
                if en == "vector":
                    i2 = eng.reduce_sum(st_sum[:, :], acc[t][:, :], AX.X)
                    C.produce("vch", i2)
                    C.wait(eng, "vch")
                    i2 = eng.tensor_scalar(st_mean[:, :], st_sum[:, :], 1.0 / D,
                                           None, ALU.mult)
                    C.produce("vch", i2)
                    C.wait(eng, "vch")
                    i2 = eng.tensor_tensor(st_msq[:, :], st_mean[:, :],
                                           st_mean[:, :], ALU.mult)
                    C.produce("vch", i2)
                    i2 = eng.tensor_add(st_var[:, :], st_sq1[:, :], st_sq2[:, :])
                    C.produce("vch", i2)
                    C.wait(eng, "vch")
                    i2 = eng.scalar_tensor_tensor(st_var[:, :], st_var[:, :],
                                                  1.0 / D, st_msq[:, :], ALU.mult,
                                                  ALU.subtract)
                    C.produce("var", i2)
                else:
                    C.produce("vch", None)
                    C.produce("vch", None)
                    C.produce("vch", None)
                    C.produce("vch", None)
                    C.produce("var", None)
                C.wait(he["scalar"], "var")
                C.wait(he["scalar"], "ntok")
                if en == "scalar":
                    i2 = eng.activation(st_lnv[:, :], st_var[:, :], AF.Ln,
                                        bias=eps_t[:, :], scale=1.0)
                    C.produce("sch", i2)
                    C.wait(eng, "sch")
                    i2 = eng.activation(st_rstd[:, :], st_lnv[:, :], AF.Exp,
                                        bias=0.0, scale=-0.5)
                    C.produce("rstd", i2)
                else:
                    C.produce("sch", None)
                    C.produce("rstd", None)
                C.wait(he["vector"], "rstd")
                C.wait(he["vector"], "lnTc")
                C.wait(he["vector"], "sq")
                if en == "vector":
                    i2 = eng.tensor_scalar(lntok[:, :], acc[t][:, :], st_mean[:, :],
                                           st_rstd[:, :], ALU.subtract, ALU.mult)
                    C.produce("ntok", i2)
                else:
                    C.produce("ntok", None)
                par = t % 2
                if mode == "blocks":
                    C.wait(he["vector"], f"lnbw{par}")
                C.wait(he["tensor"], "ntok")
                for f in range(NK):
                    sl = f % 2
                    if t == 0 and f < 2:
                        C.wait(he["tensor"], f"Rf{sl}")
                    C.wait(he["tensor"], "lnTc", C.count("lnTc") - 1)
                    if en == "tensor":
                        i2 = nc.tensor.transpose(TSL[sl],
                                                 lntok[:, f * 128:(f + 1) * 128],
                                                 identb[:, :])
                        C.produce(f"Rg{sl}", i2)
                    else:
                        C.produce(f"Rg{sl}", None)
                    C.wait(he["vector"], f"Rg{sl}")
                    if mode == "chunks":
                        cdst = lnT_dst[f][:, t * 128:(t + 1) * 128]
                    else:
                        cdst = lnblk_st[par][:, f * 128:(f + 1) * 128]
                    if en == "vector":
                        i2 = eng.tensor_copy(cdst, TSL[sl])
                        C.produce("lnTc", i2)
                    else:
                        C.produce("lnTc", None)
                    if router:
                        C.wait(he["tensor"], "lnTc")
                        if t == 0 and f == 0:
                            C.wait(he["tensor"], "Rf2")
                        if en == "tensor":
                            i2 = nc.tensor.matmul(
                                pBig2[0:8, t * 128:(t + 1) * 128],
                                routerTb.ap().rearrange("p (k e) -> p k e",
                                                        e=NE)[:, f, :],
                                cdst, start=(f == 0), stop=(f == NK - 1))
                            if t == NT - 1 and f == NK - 1:
                                C.produce("lgmm", i2)
                        elif t == NT - 1 and f == NK - 1:
                            C.produce("lgmm", None)
                if mode == "blocks":
                    C.wait(he["scalar"], "lnTc")
                    dma("scalar",
                        lnb_d[blk0 + t].rearrange("k r c -> r k c"),
                        lnblk_st[par].ap().rearrange("p (k c) -> p k c", c=128),
                        f"lnbw{par}")

        def gating():
            C.wait(he["vector"], "lgmm")
            C.wait(he["vector"], "gtp0")
            C.wait(he["vector"], "gtp1")
            if en == "vector":
                i2 = eng.tensor_copy(lg_sb[:, :], pBig2[0:8, 0:PT])
                C.produce("lgsb", i2)
                C.produce("Rf2", eng.memset(st_msq[:, :], 0.0))
            else:
                C.produce("lgsb", None)
                C.produce("Rf2", None)
            for t in range(NT):
                sl = t % 2
                C.wait(he["tensor"], "lgsb")
                if t < 2:
                    C.wait(he["tensor"], "lnTc")
                C.wait(he["tensor"], "gd", C.count("gd") - 1)
                if en == "tensor":
                    i2 = nc.tensor.transpose(GSL[sl],
                                             lg_sb[:, t * 128:(t + 1) * 128],
                                             identf[0:8, 0:8])
                    C.produce(f"gtp{sl}", i2)
                else:
                    C.produce(f"gtp{sl}", None)
                C.wait(he["vector"], f"gtp{sl}")
                if en == "vector":
                    i2 = eng.tensor_copy(g_lgtok[:, :], GSL[sl])
                    C.produce("vch", i2)
                    C.wait(eng, "vch")
                    i2 = eng.max(g_max8[:, :], g_lgtok[:, :])
                    C.produce("vch", i2)
                    C.wait(eng, "vch")
                    i2 = eng.tensor_sub(g_d[:, :], g_max8[:, 0:1], g_max8[:, 1:2])
                    C.produce("gd", i2)
                else:
                    C.produce("vch", None)
                    C.produce("vch", None)
                    C.produce("gd", None)
                C.wait(he["scalar"], "gd")
                C.wait(he["scalar"], "gtok")
                if en == "scalar":
                    i2 = eng.activation(g_s1[:, :], g_d[:, :], AF.Sigmoid)
                    C.produce("gs", i2)
                else:
                    C.produce("gs", None)
                C.wait(he["vector"], "gs")
                if en == "vector":
                    i2 = eng.tensor_scalar(g_s1[:, :], g_s1[:, :], 0.5, None,
                                           ALU.mult)
                    C.produce("vch", i2)
                    C.wait(eng, "vch")
                    i2 = eng.tensor_scalar(g_s2[:, :], g_s1[:, :], -1.0, 0.5,
                                           ALU.mult, ALU.add)
                    C.produce("vch", i2)
                    i2 = eng.tensor_scalar(g_i1[:, :], g_lgtok[:, :],
                                           g_max8[:, 0:1], None, ALU.is_ge)
                    C.produce("vch", i2)
                    i2 = eng.tensor_scalar(g_i2[:, :], g_lgtok[:, :],
                                           g_max8[:, 1:2], None, ALU.is_ge)
                    C.produce("vch", i2)
                    C.wait(eng, "vch")
                    i2 = eng.tensor_sub(g_i2[:, :], g_i2[:, :], g_i1[:, :])
                    C.produce("vch", i2)
                    i2 = eng.tensor_scalar(g_t1[:, :], g_i1[:, :], g_s1[:, :],
                                           None, ALU.mult)
                    C.produce("vch", i2)
                    C.wait(eng, "vch")
                    i2 = eng.scalar_tensor_tensor(gatetok[t][:, :], g_i2[:, :],
                                                  g_s2[:, :], g_t1[:, :], ALU.mult,
                                                  ALU.add)
                    C.produce("gtok", i2)
                else:
                    for _ in range(6):
                        C.produce("vch", None)
                    C.produce("gtok", None)

        def ffn_pass(xsrc, row0, dst, drow0, out_ch):
            ln_pass(xsrc, row0, "chunks", out_ch, lnT_dst=lnT)
            gating()
            acc_done = [[None, None] for _ in range(NT)]
            for e in range(NEX):
                par = e % 2
                gregs = (0, 1) if par == 0 else (2, 3)
                vregs = (2, 3) if par == 0 else (0, 1)
                s1, s2, s3 = (3 * e) % 2, (3 * e + 1) % 2, (3 * e + 2) % 2
                for (src, slw, wid) in ((w1_in, s1, FF), (w2_in, s2, FF)):
                    for (ch, val) in wfree_req[slw]:
                        C.wait(he["gpsimd"], ch, val)
                    dma("gpsimd",
                        wslot[slw].ap().rearrange("p (k f) -> p k f", f=FF),
                        src[e].rearrange("(k r) f -> r k f", r=128), f"wld{slw}")
                w1v = wslot[s1].ap().rearrange("p (k f) -> p k f", f=FF)
                w2v = wslot[s2].ap().rearrange("p (k f) -> p k f", f=FF)
                w3v = wslot[s3].ap().rearrange("p (j f) -> p j f", f=D)
                gb = [C.count(f"Rf{gregs[0]}"), C.count(f"Rf{gregs[1]}")]
                C.wait(he["tensor"], f"wld{s1}")
                for j in range(NJ):
                    r = gregs[j % 2]
                    if e == 0 and j < 2:
                        C.wait(he["tensor"], "lnTc")
                        C.wait(he["tensor"], "gtok")
                    C.wait(he["tensor"], f"Rf{r}")
                    for sl in range(2):
                        for k in range(NK):
                            last = (sl == 1 and k == NK - 1)
                            if en == "tensor":
                                i2 = nc.tensor.matmul(
                                    RV[r][:, sl * 512:(sl + 1) * 512],
                                    w1v[:, k, j * 128:(j + 1) * 128],
                                    lnT[k][:, sl * 512:(sl + 1) * 512],
                                    start=(k == 0), stop=(k == NK - 1))
                                if last:
                                    C.produce(f"Rg{r}", i2)
                            elif last:
                                C.produce(f"Rg{r}", None)
                    C.wait(he["scalar"], f"Rg{r}")
                    if en == "scalar":
                        i2 = eng.activation(s_h[j][:, :], RV[r], AF.Silu)
                        C.produce(f"Rf{r}", i2)
                    else:
                        C.produce(f"Rf{r}", None)
                wfree_req[s1] = [(f"Rf{gregs[1]}", C.count(f"Rf{gregs[1]}"))]
                for (ch, val) in wfree_req[s3]:
                    C.wait(he["gpsimd"], ch, val)
                dma("gpsimd",
                    wslot[s3].ap().rearrange("p (j f) -> p j f", f=D),
                    w3_in[e].rearrange("(j r) f -> r j f", r=128), f"wld{s3}")
                vb = [C.count(f"Rf{vregs[0]}"), C.count(f"Rf{vregs[1]}")]
                C.wait(he["tensor"], f"wld{s2}")
                for j in range(NJ):
                    r = vregs[j % 2]
                    C.wait(he["tensor"], f"Rf{r}")
                    for sl in range(2):
                        for k in range(NK):
                            last = (sl == 1 and k == NK - 1)
                            if en == "tensor":
                                i2 = nc.tensor.matmul(
                                    RV[r][:, sl * 512:(sl + 1) * 512],
                                    w2v[:, k, j * 128:(j + 1) * 128],
                                    lnT[k][:, sl * 512:(sl + 1) * 512],
                                    start=(k == 0), stop=(k == NK - 1))
                                if last:
                                    C.produce(f"Rg{r}", i2)
                            elif last:
                                C.produce(f"Rg{r}", None)
                    C.wait(he["vector"], f"Rg{r}")
                    C.wait(he["vector"], f"Rf{gregs[j % 2]}",
                           gb[j % 2] + j // 2 + 1)
                    if en == "vector":
                        i2 = eng.tensor_tensor(s_h[j][:, :], s_h[j][:, :], RV[r],
                                               ALU.mult)
                        C.produce(f"Rf{r}", i2)
                    else:
                        C.produce(f"Rf{r}", None)
                wfree_req[s2] = [(f"Rf{vregs[1]}", C.count(f"Rf{vregs[1]}"))]
                C.wait(he["tensor"], f"wld{s3}")
                C.wait(he["tensor"], f"Rf{vregs[0]}", vb[0] + 4)
                C.wait(he["tensor"], f"Rf{vregs[1]}", vb[1] + 4)
                for t in range(NT):
                    for foh in range(2):
                        r = gregs[(t * 2 + foh) % 2]
                        C.wait(he["tensor"], f"Rf{r}")
                        for sl in range(2):
                            for j in range(NJ):
                                last = (sl == 1 and j == NJ - 1)
                                if en == "tensor":
                                    i2 = nc.tensor.matmul(
                                        RV[r][:, sl * 512:(sl + 1) * 512],
                                        s_h[j][:, t * 128:(t + 1) * 128],
                                        w3v[:, j, foh * 1024 + sl * 512:
                                            foh * 1024 + (sl + 1) * 512],
                                        start=(j == 0), stop=(j == NJ - 1))
                                    if last:
                                        C.produce(f"Rg{r}", i2)
                                elif last:
                                    C.produce(f"Rg{r}", None)
                        C.wait(he["vector"], f"Rg{r}")
                        if en == "vector":
                            i2 = eng.scalar_tensor_tensor(
                                acc[t][:, foh * 1024:(foh + 1) * 1024], RV[r],
                                gatetok[t][:, e:e + 1],
                                acc[t][:, foh * 1024:(foh + 1) * 1024],
                                ALU.mult, ALU.add)
                            C.produce(f"Rf{r}", i2)
                        else:
                            C.produce(f"Rf{r}", None)
                        if e == NEX - 1:
                            acc_done[t][foh] = (r, C.count(f"Rf{r}"))
                rlast = gregs[(NT * 2 - 1) % 2]
                wfree_req[s3] = [(f"Rf{rlast}", C.count(f"Rf{rlast}"))]
            for t in range(NT):
                for foh in range(2):
                    r, v = acc_done[t][foh]
                    C.wait(he["sync"], f"Rf{r}", v)
                dma("sync", dst[drow0 + t * 128: drow0 + (t + 1) * 128, :],
                    acc[t][:, :], out_ch)

        def proj_sweep(w_src, dst_d, ntiles, wch):
            for (ch, val) in wfree_req[0] + wfree_req[1]:
                C.wait(he["gpsimd"], ch, val)
            dma("gpsimd", wsp.ap().rearrange("p (k f) -> p k f", f=D),
                w_src.ap().rearrange("(k r) f -> r k f", r=128), wch)
            wspv = wsp.ap().rearrange("p (k f) -> p k f", f=D)
            for t in range(ntiles):
                par = t % 2
                regs = (0, 1) if par == 0 else (2, 3)
                C.wait(he["sync"], f"Rf{regs[0]}")
                C.wait(he["sync"], f"Rf{regs[1]}")
                dma("sync", lnblk[par].ap().rearrange("p (k c) -> p k c", c=128),
                    lnb_d[t].rearrange("k r c -> r k c"), f"lnbl{par}")
                lv = lnblk[par].ap().rearrange("p (k c) -> p k c", c=128)
                C.wait(he["tensor"], wch)
                C.wait(he["tensor"], f"lnbl{par}")
                claim(regs)
                for sl in range(4):
                    for k in range(NK):
                        last = (sl == 3 and k == NK - 1)
                        if en == "tensor":
                            i2 = nc.tensor.matmul(
                                PBIG[par][:, sl * 512:(sl + 1) * 512],
                                lv[:, k, :],
                                wspv[:, k, sl * 512:(sl + 1) * 512],
                                start=(k == 0), stop=(k == NK - 1))
                            if last:
                                C.produce(f"Rg{regs[0]}", i2)
                        elif last:
                            C.produce(f"Rg{regs[0]}", None)
                C.wait(he["vector"], f"Rg{regs[0]}")
                C.wait(he["vector"], f"pjw{par}")
                if en == "vector":
                    i2 = eng.tensor_copy(b1st[par][:, 0:1024], RV[regs[0]])
                    C.produce(f"Rf{regs[0]}", i2)
                    i2 = eng.tensor_copy(b1st[par][:, 1024:2048], RV[regs[1]])
                    C.produce(f"Rf{regs[1]}", i2)
                else:
                    C.produce(f"Rf{regs[0]}", None)
                    C.produce(f"Rf{regs[1]}", None)
                C.wait(he["scalar"], f"Rf{regs[0]}")
                C.wait(he["scalar"], f"Rf{regs[1]}")
                dma("scalar", dst_d[t * 128:(t + 1) * 128, :], b1st[par][:, :],
                    f"pjw{par}")
            rel = []
            for pp in (0, 1):
                rr_ = (0, 1) if pp == 0 else (2, 3)
                rel.append((f"Rf{rr_[0]}", C.count(f"Rf{rr_[0]}")))
                rel.append((f"Rf{rr_[1]}", C.count(f"Rf{rr_[1]}")))
            wfree_req[0] = rel
            wfree_req[1] = rel

        def scores_epair(ep):
            qslot = ep % 2
            for rr_ in range(4):
                C.wait(he["sync"], f"Rf{rr_}")
            qv = qep[qslot].ap().rearrange("p (sc c) -> p sc c", c=256)
            for g in range(4):
                dma("sync",
                    qep[qslot].ap().rearrange("p (g sc c) -> p g sc c",
                                              g=4, c=256)[:, g, :, :],
                    q_d.ap()[g * 1024:(g + 1) * 1024, ep * 256:(ep + 1) * 256]
                       .rearrange("(sc r) c -> r sc c", r=128),
                    f"qepl{qslot}")
            C.wait(he["tensor"], f"qepl{qslot}")
            claim((0, 1, 2, 3))
            for sc in range(NSC):
                ks = sc % 4
                C.wait(he["sync"], f"krf{ks}")
                if sc < 4:
                    for rr_ in range(4):
                        C.wait(he["sync"], f"Rf{rr_}")
                dma("sync", kring[ks][:, :], k_d[sc * 128:(sc + 1) * 128, :],
                    f"krl{ks}")
                C.wait(he["tensor"], f"krl{ks}")
                for e01 in range(2):
                    for sl in range(4):
                        last_k = (e01 == 1 and sl == 3)
                        last_all = (sc == NSC - 1 and sl == 3)
                        if en == "tensor":
                            i2 = nc.tensor.matmul(
                                PBIG[e01][:, sl * 512:(sl + 1) * 512],
                                qv[:, sc, e01 * 128:(e01 + 1) * 128],
                                kring[ks][:, sl * 512:(sl + 1) * 512],
                                start=(sc == 0), stop=(sc == NSC - 1))
                            if last_k and not last_all:
                                C.produce(f"krf{ks}", i2)
                            if last_all:
                                C.produce(f"Rg{2 * e01}", i2)
                                if e01 == 1:
                                    C.produce(f"krf{ks}", eng.nop())
                        else:
                            if last_k and not last_all:
                                C.produce(f"krf{ks}", None)
                            if last_all:
                                C.produce(f"Rg{2 * e01}", None)
                                if e01 == 1:
                                    C.produce(f"krf{ks}", None)
            for e01 in range(2):
                et = ep * 2 + e01
                C.wait(he["vector"], f"Rg{2 * e01}")
                C.wait(he["vector"], "smexp")
                if en == "vector":
                    i2 = eng.reduce_max(sm_m[:, :], PBIG[e01][:, :], AX.X)
                    C.produce("vch", i2)
                    C.wait(eng, "vch")
                    i2 = eng.tensor_scalar(sm_b[:, :], sm_m[:, :], -0.125, None,
                                           ALU.mult)
                    C.produce("smb", i2)
                else:
                    C.produce("vch", None)
                    C.produce("smb", None)
                C.wait(he["scalar"], "smb")
                C.wait(he["scalar"], f"pw{e01}")
                if en == "scalar":
                    i2 = eng.activation(pstage[e01][:, :], PBIG[e01][:, :], AF.Exp,
                                        bias=sm_b[:, :], scale=0.125,
                                        accum_out=sm_s[:, :])
                    C.produce("smexp", i2)
                else:
                    C.produce("smexp", None)
                C.wait(he["vector"], "smexp")
                if en == "vector":
                    i2 = eng.reciprocal(sm_r[:, :], sm_s[:, :])
                    C.produce(f"Rf{2 * e01 + 1}", i2)
                    C.wait(eng, f"Rf{2 * e01 + 1}")
                    i2 = eng.tensor_scalar(pstage[e01][:, :], pstage[e01][:, :],
                                           sm_r[:, :], None, ALU.mult)
                    C.produce(f"Rf{2 * e01}", i2)
                else:
                    C.produce(f"Rf{2 * e01 + 1}", None)
                    C.produce(f"Rf{2 * e01}", None)
                C.wait(he["sync"], f"Rf{2 * e01}")
                dma("sync", P_d[et][:, :], pstage[e01][:, :], f"pw{e01}")

        def wtilde():
            dma("gpsimd", wob.ap().rearrange("p (k f) -> p k f", f=D),
                wo_in.ap().rearrange("(k r) f -> r k f", r=128), "wold")
            wov = wob.ap().rearrange("p (k f) -> p k f", f=D)
            for ft in range(NK):
                par = ft % 2
                regs = (0, 1) if par == 0 else (2, 3)
                C.wait(he["sync"], f"Rf{regs[0]}")
                C.wait(he["sync"], f"Rf{regs[1]}")
                for g in range(2):
                    dma("sync",
                        pblk[par].ap().rearrange("p (g e c) -> p g e c",
                                                 g=2, c=128)[:, g, :, :],
                        P_d.ap()[g * 8:(g + 1) * 8, :, ft * 128:(ft + 1) * 128]
                           .rearrange("e r c -> r e c"),
                        f"pbl{par}")
                pbv = pblk[par].ap().rearrange("p (e c) -> p e c", c=128)
                C.wait(he["tensor"], "wold")
                C.wait(he["tensor"], f"pbl{par}")
                claim(regs)
                for sl in range(4):
                    for e in range(NK):
                        last = (sl == 3 and e == NK - 1)
                        if en == "tensor":
                            i2 = nc.tensor.matmul(
                                PBIG[par][:, sl * 512:(sl + 1) * 512],
                                pbv[:, e, :],
                                wov[:, e, sl * 512:(sl + 1) * 512],
                                start=(e == 0), stop=(e == NK - 1))
                            if last:
                                C.produce(f"Rg{regs[0]}", i2)
                        elif last:
                            C.produce(f"Rg{regs[0]}", None)
                C.wait(he["vector"], f"Rg{regs[0]}")
                C.wait(he["vector"], f"wtw{par}")
                if en == "vector":
                    i2 = eng.tensor_copy(wtstage[par][:, 0:1024], RV[regs[0]])
                    C.produce(f"Rf{regs[0]}", i2)
                    i2 = eng.tensor_copy(wtstage[par][:, 1024:2048], RV[regs[1]])
                    C.produce(f"Rf{regs[1]}", i2)
                else:
                    C.produce(f"Rf{regs[0]}", None)
                    C.produce(f"Rf{regs[1]}", None)
                C.wait(he["scalar"], f"Rf{regs[0]}")
                C.wait(he["scalar"], f"Rf{regs[1]}")
                dma("scalar", Wt_d[ft][:, :], wtstage[par][:, :], f"wtw{par}")

        def av_apply():
            vtv = vTb.ap().rearrange("p (k s) -> p k s", s=D)
            for t in range(NK):
                par = t % 2
                C.wait(he["sync"], "vtbc")
                dma("sync", vtr[par][:, :], vt_d[t * 128:(t + 1) * 128, :],
                    f"vtrl{par}")
                C.wait(he["tensor"], f"vtrl{par}")
                for f in range(NK):
                    sl = f % 2
                    if t == 0 and f < 2:
                        C.wait(he["tensor"], f"Rf{sl}")
                    C.wait(he["tensor"], "vtbc", C.count("vtbc") - 1)
                    if en == "tensor":
                        i2 = nc.tensor.transpose(TSL[sl],
                                                 vtr[par][:, f * 128:(f + 1) * 128],
                                                 identb[:, :])
                        C.produce(f"Rg{sl}", i2)
                    else:
                        C.produce(f"Rg{sl}", None)
                    C.wait(he["vector"], f"Rg{sl}")
                    if en == "vector":
                        i2 = eng.tensor_copy(vtv[:, f, t * 128:(t + 1) * 128],
                                             TSL[sl])
                        C.produce("vtbc", i2)
                    else:
                        C.produce("vtbc", None)
            for f in range(NK):
                dma("sync", wtb[:, f * D:(f + 1) * D], Wt_d[f][:, :], "wtl")
            wtv = wtb.ap().rearrange("p (k f) -> p k f", f=D)
            for t in range(NK):
                par = t % 2
                regs = (0, 1) if par == 0 else (2, 3)
                C.wait(he["sync"], f"Rf{regs[0]}")
                C.wait(he["sync"], f"Rf{regs[1]}")
                dma("sync", x1r[par][:, :], X1_d[t * 128:(t + 1) * 128, :],
                    f"x1l{par}")
                C.wait(he["tensor"], "wtl")
                C.wait(he["tensor"], "vtbc")
                claim(regs)
                for sl in range(4):
                    for f in range(NK):
                        last = (sl == 3 and f == NK - 1)
                        if en == "tensor":
                            i2 = nc.tensor.matmul(
                                PBIG[par][:, sl * 512:(sl + 1) * 512],
                                vtv[:, f, t * 128:(t + 1) * 128],
                                wtv[:, f, sl * 512:(sl + 1) * 512],
                                start=(f == 0), stop=(f == NK - 1))
                            if last:
                                C.produce(f"Rg{regs[0]}", i2)
                        elif last:
                            C.produce(f"Rg{regs[0]}", None)
                C.wait(he["vector"], f"Rg{regs[0]}")
                C.wait(he["vector"], f"x1l{par}")
                C.wait(he["vector"], f"x2w{par}")
                if en == "vector":
                    i2 = eng.tensor_tensor(x2st[par][:, 0:1024],
                                           x1r[par][:, 0:1024], RV[regs[0]],
                                           ALU.add)
                    C.produce(f"Rf{regs[0]}", i2)
                    i2 = eng.tensor_tensor(x2st[par][:, 1024:2048],
                                           x1r[par][:, 1024:2048], RV[regs[1]],
                                           ALU.add)
                    C.produce(f"Rf{regs[1]}", i2)
                else:
                    C.produce(f"Rf{regs[0]}", None)
                    C.produce(f"Rf{regs[1]}", None)
                C.wait(he["scalar"], f"Rf{regs[0]}")
                C.wait(he["scalar"], f"Rf{regs[1]}")
                dma("scalar", X2_d[t * 128:(t + 1) * 128, :], x2st[par][:, :],
                    f"x2w{par}")

        if "A" in phases:
            for p in range(A_PASSES):
                ffn_pass(x_in, p * PT, X1_d, p * PT, "X1w")
            barrier(drains=("X1w",))

        if "B" in phases:
            for p in range(4):
                ln_pass(X1_d, p * PT, "blocks", "ntok", blk0=p * NT)
            barrier(drains=("lnbw0", "lnbw1"))

            proj_sweep(wq_in, q_d, NSC, "wldq")
            if B_SUB >= 2:
                proj_sweep(wk_in, k_d, NSC, "wldk")
                proj_sweep(wv_in, vt_d, NK, "wldv")
            barrier(drains=("pjw0", "pjw1"))

            if B_SUB >= 3:
                for ep in range(8):
                    scores_epair(ep)
                barrier(drains=("pw0", "pw1"))

                wtilde()
                barrier(drains=("wtw0", "wtw1"))

            if B_SUB >= 4:
                av_apply()
                barrier(drains=("x2w0", "x2w1"))

        if "C" in phases:
            for p in range(C_PASSES):
                ffn_pass(X2_d, p * PT, y_out, p * PT, "yw")
            C.wait(he["sync"], "yw")

        dbg_srcs = dict(X1=X1_d, X2=X2_d, qd=q_d, kd=k_d, vt=vt_d, Pd=P_d,
                        Wt=Wt_d, lnb=lnb_d)
        if "st" in dbg_outs:
            C.wait(he["gpsimd"], "X1w")
            for i, t_ in enumerate((st_sum, st_sq1, st_sq2, st_mean, st_msq,
                                    st_var, st_rstd, eps_t)):
                dma("gpsimd", dbg_outs["st"][i], t_[:, :], "dbgw")
        if "acc0" in dbg_outs:
            C.wait(he["gpsimd"], "X1w")
            dma("gpsimd", dbg_outs["acc0"][:, :], acc[0][:, :], "dbgw")
        for nm, srcs in (("lnT", lnT), ("sh", s_h), ("gate", gatetok)):
            if nm in dbg_outs:
                C.wait(he["gpsimd"], "X1w")
                for i, t_ in enumerate(srcs):
                    dma("gpsimd", dbg_outs[nm][i], t_[:, :], "dbgw")
        if "lg" in dbg_outs:
            C.wait(he["gpsimd"], "X1w")
            dma("gpsimd", dbg_outs["lg"][:, :], lg_sb[:, :], "dbgw")
        if dbg_outs:
            for ch in ("X1w", "lnbw0", "lnbw1", "pjw0", "pjw1", "pw0", "pw1",
                       "wtw0", "wtw1", "x2w0", "x2w1", "yw"):
                C.wait(he["gpsimd"], ch)
        for name, outd in dbg_outs.items():
            if name not in dbg_srcs:
                continue
            src = dbg_srcs[name]
            if name in ("Pd", "Wt", "lnb"):
                dma("gpsimd", outd.ap(), src.ap(), "dbgw")
            else:
                dma("gpsimd", outd[:, :], src[:, :], "dbgw")
        C.wait(he["gpsimd"], "dbgw")
        C.wait(he["gpsimd"], "yw")

    with nc.Block() as block:
        @block.sync
        def _(e):
            emit("sync", e, Chans(sems))

        @block.scalar
        def _(e):
            emit("scalar", e, Chans(sems))

        @block.vector
        def _(e):
            emit("vector", e, Chans(sems))

        @block.tensor
        def _(e):
            emit("tensor", e, Chans(sems))

        @block.gpsimd
        def _(e):
            emit("gpsimd", e, Chans(sems))

    es.close()
    return nc


_NC_CACHE = {}


def kernel(L0_attn_x, L0_attn_Wq, L0_attn_Wk, L0_attn_Wv, L0_attn_Wo,
           L0_ffn_router_w, L0_ffn_w1, L0_ffn_w2, L0_ffn_w3):
    x = np.asarray(L0_attn_x, dtype=np.float32)
    Wq = np.ascontiguousarray(np.asarray(L0_attn_Wq, np.float32))
    Wk = np.ascontiguousarray(np.asarray(L0_attn_Wk, np.float32))
    Wv = np.ascontiguousarray(np.asarray(L0_attn_Wv, np.float32))
    Wo = np.ascontiguousarray(np.asarray(L0_attn_Wo, np.float32))
    rT = np.ascontiguousarray(np.asarray(L0_ffn_router_w, np.float32).T)
    w1 = np.ascontiguousarray(np.asarray(L0_ffn_w1, np.float32))
    w2 = np.ascontiguousarray(np.asarray(L0_ffn_w2, np.float32))
    w3 = np.ascontiguousarray(np.asarray(L0_ffn_w3, np.float32))
    ident = np.eye(128, dtype=np.float32)

    if "nc" not in _NC_CACHE:
        _NC_CACHE["nc"] = build()
    nc = _NC_CACHE["nc"]
    in_maps = []
    for c in range(8):
        b, h = c // 2, c % 2
        own = x[b, h * 2048:(h + 1) * 2048, :]
        oth = x[b, (1 - h) * 2048:(2 - h) * 2048, :]
        xp = np.ascontiguousarray(np.concatenate([own, oth], axis=0))
        in_maps.append({
            "x_in": xp, "wq_in": Wq, "wk_in": Wk, "wv_in": Wv, "wo_in": Wo,
            "rT_in": rT, "w1_in": w1, "w2_in": w2, "w3_in": w3, "id_in": ident,
        })
    res = run_bass_kernel_spmd(nc, in_maps, core_ids=list(range(8)))
    out = np.empty((4, 4096, 2048), dtype=np.float32)
    for c in range(8):
        b, h = c // 2, c % 2
        out[b, h * 2048:(h + 1) * 2048, :] = res.results[c]["y"]
    return out


# version 04
# speedup vs baseline: 1.5044x; 1.5044x over previous
import numpy as np
from contextlib import ExitStack

import concourse.bass as bass
import concourse.mybir as mybir
from concourse.bass_utils import run_bass_kernel_spmd

F32 = mybir.dt.float32
BF16 = mybir.dt.bfloat16
AF = mybir.ActivationFunctionType
ALU = mybir.AluOpType
AX = mybir.AxisListType

D = 2048
FF = 1024
NE = 8
OWN = 2048
PT = 1024
NT = PT // 128
NK = D // 128
NJ = FF // 128
NSC = OWN // 128
EPS = 1e-5
A_PASSES = 2
C_PASSES = 2
NEX = NE
B_SUB = 4
ENGS = ("sync", "scalar", "vector", "tensor", "gpsimd")
RGROUPS = [[0, 1], [2, 3], [4, 5], [6, 7]]


class Chans:
    def __init__(self, sems):
        self.sems = sems
        self.map = {}
        self.counts = {}
        self.nused = 0

    def _get(self, name):
        if name not in self.map:
            if self.nused >= len(self.sems):
                raise RuntimeError("out of semaphores")
            self.map[name] = self.sems[self.nused]
            self.nused += 1
            self.counts[name] = 0
        return self.map[name]

    def produce(self, name, inst, k=1):
        sem = self._get(name)
        self.counts[name] += k
        if inst is not None:
            inst.then_inc(sem, k)

    def wait(self, eng, name, val=None):
        sem = self._get(name)
        v = self.counts[name] if val is None else val
        if eng is not None and v > 0:
            eng.wait_ge(sem, v)

    def count(self, name):
        self._get(name)
        return self.counts[name]


def build(phases=("A", "B", "C"), dbg=()):
    nc = bass.Bass("TRN2", target_bir_lowering=False, debug=False,
                   detect_race_conditions=False)

    x_in = nc.dram_tensor("x_in", [OWN, D], F32, kind="ExternalInput")
    wq_in = nc.dram_tensor("wq_in", [D, D], F32, kind="ExternalInput")
    wk_in = nc.dram_tensor("wk_in", [D, D], F32, kind="ExternalInput")
    wv_in = nc.dram_tensor("wv_in", [D, D], F32, kind="ExternalInput")
    wo_in = nc.dram_tensor("wo_in", [D, D], F32, kind="ExternalInput")
    rT_in = nc.dram_tensor("rT_in", [D, NE], F32, kind="ExternalInput")
    w1_in = nc.dram_tensor("w1_in", [NE, D, FF], F32, kind="ExternalInput")
    w2_in = nc.dram_tensor("w2_in", [NE, D, FF], F32, kind="ExternalInput")
    w3_in = nc.dram_tensor("w3_in", [NE, FF, D], F32, kind="ExternalInput")
    id_in = nc.dram_tensor("id_in", [128, 128], F32, kind="ExternalInput")
    y_out = nc.dram_tensor("y", [OWN, D], F32, kind="ExternalOutput")
    dbg_outs = {}
    for name, shp, dt in (("lnb", [NSC, NK, 128, 128], BF16),
                          ("st", [8, 128, 1], F32), ("acc0", [128, D], F32),
                          ("lnT", [NK, 128, PT], BF16), ("sh", [NJ, 128, PT], BF16),
                          ("gate", [NT, 128, NE], F32), ("lg", [8, PT], F32),
                          ("X1", [OWN, D], F32), ("X2", [OWN, D], F32),
                          ("qd", [OWN, D], BF16), ("kd", [OWN, D], BF16),
                          ("vt", [OWN, D], BF16), ("Pd", [NK, 128, D], BF16),
                          ("Wt", [NK, 128, D], BF16)):
        if name in dbg:
            dbg_outs[name] = nc.dram_tensor("dbg_" + name, shp, dt,
                                            kind="ExternalOutput")

    X1_d = nc.dram_tensor("X1_d", [OWN, D], F32)
    X2_d = nc.dram_tensor("X2_d", [OWN, D], F32)
    lnb_d = nc.dram_tensor("lnb_d", [NSC, NK, 128, 128], BF16)
    q_d = nc.dram_tensor("q_d", [OWN, D], BF16)
    k_d = nc.dram_tensor("k_d", [OWN, D], BF16)
    vt_d = nc.dram_tensor("vt_d", [OWN, D], BF16)
    Sp_d = nc.dram_tensor("Sp_d", [D, D], F32)
    Sf_d = nc.dram_tensor("Sf_d", [D, D], F32)
    P_d = nc.dram_tensor("P_d", [NK, 128, D], BF16)
    Wt_d = nc.dram_tensor("Wt_d", [NK, 128, D], BF16)

    SB_TOP = nc.sbuf_top
    ARENA_BYTES = ((SB_TOP - nc.sbuf_base - 8192) // 32) * 32
    arena_t = nc.alloc_sbuf_tensor("big_arena", [128, ARENA_BYTES],
                                   mybir.dt.uint8)
    ARENA_BASE = nc.lookup_mloc(arena_t).addr

    class Arena:
        def __init__(self, start):
            self.off = (start + 31) & ~31

        def al(self, name, shape, dtype):
            nbytes = int(np.prod(shape[1:])) * mybir.dt.size(dtype)
            self.off = (self.off + 31) & ~31
            t = nc.alloc_sbuf_tensor_at(name, shape, dtype, offset=self.off)
            self.off += nbytes
            assert self.off <= SB_TOP, f"SBUF overflow at {name}: {self.off}>{SB_TOP}"
            return t

    G = Arena(ARENA_BASE)
    identb = G.al("identb", [128, 128], BF16)
    identf = G.al("identf", [128, 128], F32)
    routerTb = G.al("routerTb", [128, NK * NE], BF16)
    st_sum = G.al("st_sum", [128, 1], F32)
    st_sq1 = G.al("st_sq1", [128, 1], F32)
    st_sq2 = G.al("st_sq2", [128, 1], F32)
    st_mean = G.al("st_mean", [128, 1], F32)
    st_msq = G.al("st_msq", [128, 1], F32)
    st_var = G.al("st_var", [128, 1], F32)
    st_rstd = G.al("st_rstd", [128, 1], F32)
    st_lnv = G.al("st_lnv", [128, 1], F32)
    eps_t = G.al("eps_t", [128, 1], F32)
    sm_m = G.al("sm_m", [128, 1], F32)
    sm_b = G.al("sm_b", [128, 1], F32)
    sm_s = G.al("sm_s", [128, 1], F32)
    sm_r = G.al("sm_r", [128, 1], F32)
    g_max8 = G.al("g_max8", [128, 8], F32)
    g_d = G.al("g_d", [128, 1], F32)
    g_s1 = G.al("g_s1", [128, 1], F32)
    g_s2 = G.al("g_s2", [128, 1], F32)
    g_i1 = G.al("g_i1", [128, NE], F32)
    g_i2 = G.al("g_i2", [128, NE], F32)
    g_t1 = G.al("g_t1", [128, NE], F32)
    g_lgtok = G.al("g_lgtok", [128, NE], F32)
    gatetok = [G.al(f"gatetok{t}", [128, NE], F32) for t in range(NT)]
    PHB = G.off

    FA = Arena(PHB)
    acc = [FA.al(f"acc{t}", [128, D], F32) for t in range(NT)]
    lntok = FA.al("lntok", [128, D], BF16)
    lg_sb = FA.al("lg_sb", [8, PT], F32)
    LNT_OFF = FA.off
    lnT = [FA.al(f"lnT{k}", [128, PT], BF16) for k in range(NK)]
    WS_OFF = FA.off
    wslot = [FA.al(f"wslot{i}", [128, 16 * FF], BF16) for i in range(2)]
    s_h = [FA.al(f"s_h{j}", [128, PT], BF16) for j in range(NJ)]
    FFN_TOP = FA.off
    assert FFN_TOP <= SB_TOP

    B1 = Arena(LNT_OFF)
    lnblk_st = [B1.al(f"lnbst{i}", [128, NK * 128], BF16) for i in range(2)]
    lnblk = [B1.al(f"lnblk{i}", [128, NK * 128], BF16) for i in range(2)]
    b1st = [B1.al(f"b1st{i}", [128, D], BF16) for i in range(2)]
    assert B1.off <= WS_OFF
    wsp = nc.alloc_sbuf_tensor_at("wsp", [128, NK * D], BF16, offset=WS_OFF)

    SC = Arena(PHB)
    kring = [SC.al(f"kring{i}", [128, D], BF16) for i in range(4)]
    qep = [SC.al(f"qep{i}", [128, NSC * 256], BF16) for i in range(2)]
    pstage = [SC.al(f"pst{i}", [128, D], F32) for i in range(2)]
    pexp = [SC.al(f"pexp{i}", [128, D], BF16) for i in range(2)]
    smm = [SC.al(f"smm{i}", [128, 1], F32) for i in range(2)]
    smb = [SC.al(f"smb{i}", [128, 1], F32) for i in range(2)]
    sms = [SC.al(f"sms{i}", [128, 1], F32) for i in range(2)]
    smr = [SC.al(f"smr{i}", [128, 1], F32) for i in range(2)]
    assert SC.off <= LNT_OFF, f"SC clashes with proj region: {SC.off}>{LNT_OFF}"

    W3 = Arena(PHB)
    wob = W3.al("wob", [128, NK * D], BF16)
    pblk = [W3.al(f"pblk{i}", [128, NK * 128], BF16) for i in range(2)]
    wtstage = [W3.al(f"wtst{i}", [128, D], BF16) for i in range(2)]
    assert W3.off <= SB_TOP

    AV = Arena(PHB)
    vTb = AV.al("vTb", [128, NK * D], BF16)
    wtb = AV.al("wtb", [128, NK * D], BF16)
    vtr = [AV.al(f"vtr{i}", [128, D], BF16) for i in range(2)]
    x1r = [AV.al(f"x1r{i}", [128, D], F32) for i in range(2)]
    x2st = [AV.al(f"x2st{i}", [128, D], F32) for i in range(2)]
    assert AV.off <= SB_TOP

    pBig1 = nc.alloc_psum_tensor("pBig1", [128, 2048], F32)
    pBig2 = nc.alloc_psum_tensor("pBig2", [128, 2048], F32)
    pB1b = pBig1.bitcast(BF16)
    RV = [pBig1[:, 0:1024], pBig1[:, 1024:2048],
          pBig2[:, 0:1024], pBig2[:, 1024:2048]]
    PBIG = [pBig1, pBig2]
    TSL = [pB1b[:, 0:128], pB1b[:, 2048:2048 + 128]]
    GSL = [pBig1[:, 0:8], pBig1[:, 1024:1024 + 8]]

    es = ExitStack()
    sems = [es.enter_context(nc.semaphore(f"c{i}")) for i in range(98)]

    def emit(en, eng, C: Chans):
        he = {n: (eng if en == n else None) for n in ENGS}
        barctr = [0]

        def dma(q, dst, src, ch):
            if en == q:
                C.produce(ch, eng.dma_start(out=dst, in_=src), 16)
            else:
                C.produce(ch, None, 16)

        def claim(regs):
            for r in regs:
                C.wait(he["tensor"], f"Rf{r}")

        def group_done(regs, inst, extra=()):
            if inst is not None:
                C.produce(f"Rg{regs[0]}", inst)
                for x in extra:
                    C.produce(x, inst)
            else:
                C.produce(f"Rg{regs[0]}", None)
                for x in extra:
                    C.produce(x, None)

        def free_regs(regs, inst, extra=()):
            if inst is not None:
                for r in regs:
                    C.produce(f"Rf{r}", inst)
                for x in extra:
                    C.produce(x, inst)
            else:
                for r in regs:
                    C.produce(f"Rf{r}", None)
                for x in extra:
                    C.produce(x, None)

        def barrier(drains=()):
            for ch in drains:
                C.wait(eng, ch)
            sem = C._get("bar")
            eng.sem_inc(sem, 1)
            barctr[0] += 1
            eng.wait_ge(sem, 5 * barctr[0])

        wfree_req = {0: [], 1: []}

        dma("gpsimd", identb[:, :], id_in[:, :], "constg")
        dma("gpsimd",
            routerTb.ap().rearrange("p (k e) -> p k e", e=NE),
            rT_in.ap().rearrange("(k r) e -> r k e", r=128), "constg")
        dma("sync", identf[:, :], id_in[:, :], "consts")
        for n in ("tensor", "vector", "scalar"):
            C.wait(he[n], "constg")
            C.wait(he[n], "consts")
        if en == "vector":
            C.produce("const1", eng.memset(eps_t[:, :], EPS), 1)
        else:
            C.produce("const1", None, 1)
        C.wait(he["scalar"], "const1")

        def ln_pass(xsrc, row0, mode, guard_ch, blk0=0, lnT_dst=None):
            router = mode == "chunks"
            for t in range(NT):
                C.wait(he["sync"], guard_ch)
                dma("sync", acc[t][:, :],
                    xsrc[row0 + t * 128: row0 + (t + 1) * 128, :], f"xin{t}")
                C.wait(he["scalar"], f"xin{t}")
                C.wait(he["scalar"], "var")
                C.wait(he["scalar"], "lnTc")
                if en == "scalar":
                    i2 = eng.activation(lntok[:, 0:1024], acc[t][:, 0:1024],
                                        AF.Square, accum_out=st_sq1[:, :])
                    C.produce("sch", i2)
                    C.wait(eng, "sch")
                    i2 = eng.activation(lntok[:, 1024:2048], acc[t][:, 1024:2048],
                                        AF.Square, accum_out=st_sq2[:, :])
                    C.produce("sq", i2)
                else:
                    C.produce("sch", None)
                    C.produce("sq", None)
                C.wait(he["vector"], f"xin{t}")
                C.wait(he["vector"], "sq")
                if en == "vector":
                    i2 = eng.reduce_sum(st_sum[:, :], acc[t][:, :], AX.X)
                    C.produce("vch", i2)
                    C.wait(eng, "vch")
                    i2 = eng.tensor_scalar(st_mean[:, :], st_sum[:, :], 1.0 / D,
                                           None, ALU.mult)
                    C.produce("vch", i2)
                    C.wait(eng, "vch")
                    i2 = eng.tensor_tensor(st_msq[:, :], st_mean[:, :],
                                           st_mean[:, :], ALU.mult)
                    C.produce("vch", i2)
                    i2 = eng.tensor_add(st_var[:, :], st_sq1[:, :], st_sq2[:, :])
                    C.produce("vch", i2)
                    C.wait(eng, "vch")
                    i2 = eng.scalar_tensor_tensor(st_var[:, :], st_var[:, :],
                                                  1.0 / D, st_msq[:, :], ALU.mult,
                                                  ALU.subtract)
                    C.produce("var", i2)
                else:
                    C.produce("vch", None)
                    C.produce("vch", None)
                    C.produce("vch", None)
                    C.produce("vch", None)
                    C.produce("var", None)
                C.wait(he["scalar"], "var")
                C.wait(he["scalar"], "ntok")
                if en == "scalar":
                    i2 = eng.activation(st_lnv[:, :], st_var[:, :], AF.Ln,
                                        bias=eps_t[:, :], scale=1.0)
                    C.produce("sch", i2)
                    C.wait(eng, "sch")
                    i2 = eng.activation(st_rstd[:, :], st_lnv[:, :], AF.Exp,
                                        bias=0.0, scale=-0.5)
                    C.produce("rstd", i2)
                else:
                    C.produce("sch", None)
                    C.produce("rstd", None)
                C.wait(he["vector"], "rstd")
                C.wait(he["vector"], "lnTc")
                C.wait(he["vector"], "sq")
                if en == "vector":
                    i2 = eng.tensor_scalar(lntok[:, :], acc[t][:, :], st_mean[:, :],
                                           st_rstd[:, :], ALU.subtract, ALU.mult)
                    C.produce("ntok", i2)
                else:
                    C.produce("ntok", None)
                par = t % 2
                if mode == "blocks":
                    C.wait(he["vector"], f"lnbw{par}")
                C.wait(he["tensor"], "ntok")
                for f in range(NK):
                    sl = f % 2
                    if t == 0 and f < 2:
                        C.wait(he["tensor"], f"Rf{sl}")
                    C.wait(he["tensor"], "lnTc", C.count("lnTc") - 1)
                    if en == "tensor":
                        i2 = nc.tensor.transpose(TSL[sl],
                                                 lntok[:, f * 128:(f + 1) * 128],
                                                 identb[:, :])
                        C.produce(f"Rg{sl}", i2)
                    else:
                        C.produce(f"Rg{sl}", None)
                    C.wait(he["vector"], f"Rg{sl}")
                    if mode == "chunks":
                        cdst = lnT_dst[f][:, t * 128:(t + 1) * 128]
                    else:
                        cdst = lnblk_st[par][:, f * 128:(f + 1) * 128]
                    if en == "vector":
                        i2 = eng.tensor_copy(cdst, TSL[sl])
                        C.produce("lnTc", i2)
                    else:
                        C.produce("lnTc", None)
                    if router:
                        C.wait(he["tensor"], "lnTc")
                        if t == 0 and f == 0:
                            C.wait(he["tensor"], "Rf2")
                        if en == "tensor":
                            i2 = nc.tensor.matmul(
                                pBig2[0:8, t * 128:(t + 1) * 128],
                                routerTb.ap().rearrange("p (k e) -> p k e",
                                                        e=NE)[:, f, :],
                                cdst, start=(f == 0), stop=(f == NK - 1))
                            if t == NT - 1 and f == NK - 1:
                                C.produce("lgmm", i2)
                        elif t == NT - 1 and f == NK - 1:
                            C.produce("lgmm", None)
                if mode == "blocks":
                    C.wait(he["scalar"], "lnTc")
                    dma("scalar",
                        lnb_d[blk0 + t].rearrange("k r c -> r k c"),
                        lnblk_st[par].ap().rearrange("p (k c) -> p k c", c=128),
                        f"lnbw{par}")

        def gating():
            C.wait(he["vector"], "lgmm")
            C.wait(he["vector"], "gtp0")
            C.wait(he["vector"], "gtp1")
            if en == "vector":
                i2 = eng.tensor_copy(lg_sb[:, :], pBig2[0:8, 0:PT])
                C.produce("lgsb", i2)
                C.produce("Rf2", eng.memset(st_msq[:, :], 0.0))
            else:
                C.produce("lgsb", None)
                C.produce("Rf2", None)
            for t in range(NT):
                sl = t % 2
                C.wait(he["tensor"], "lgsb")
                if t < 2:
                    C.wait(he["tensor"], "lnTc")
                C.wait(he["tensor"], "gd", C.count("gd") - 1)
                if en == "tensor":
                    i2 = nc.tensor.transpose(GSL[sl],
                                             lg_sb[:, t * 128:(t + 1) * 128],
                                             identf[0:8, 0:8])
                    C.produce(f"gtp{sl}", i2)
                else:
                    C.produce(f"gtp{sl}", None)
                C.wait(he["vector"], f"gtp{sl}")
                if en == "vector":
                    i2 = eng.tensor_copy(g_lgtok[:, :], GSL[sl])
                    C.produce("vch", i2)
                    C.wait(eng, "vch")
                    i2 = eng.max(g_max8[:, :], g_lgtok[:, :])
                    C.produce("vch", i2)
                    C.wait(eng, "vch")
                    i2 = eng.tensor_sub(g_d[:, :], g_max8[:, 0:1], g_max8[:, 1:2])
                    C.produce("gd", i2)
                else:
                    C.produce("vch", None)
                    C.produce("vch", None)
                    C.produce("gd", None)
                C.wait(he["scalar"], "gd")
                C.wait(he["scalar"], "gtok")
                if en == "scalar":
                    i2 = eng.activation(g_s1[:, :], g_d[:, :], AF.Sigmoid)
                    C.produce("gs", i2)
                else:
                    C.produce("gs", None)
                C.wait(he["vector"], "gs")
                if en == "vector":
                    i2 = eng.tensor_scalar(g_s1[:, :], g_s1[:, :], 0.5, None,
                                           ALU.mult)
                    C.produce("vch", i2)
                    C.wait(eng, "vch")
                    i2 = eng.tensor_scalar(g_s2[:, :], g_s1[:, :], -1.0, 0.5,
                                           ALU.mult, ALU.add)
                    C.produce("vch", i2)
                    i2 = eng.tensor_scalar(g_i1[:, :], g_lgtok[:, :],
                                           g_max8[:, 0:1], None, ALU.is_ge)
                    C.produce("vch", i2)
                    i2 = eng.tensor_scalar(g_i2[:, :], g_lgtok[:, :],
                                           g_max8[:, 1:2], None, ALU.is_ge)
                    C.produce("vch", i2)
                    C.wait(eng, "vch")
                    i2 = eng.tensor_sub(g_i2[:, :], g_i2[:, :], g_i1[:, :])
                    C.produce("vch", i2)
                    i2 = eng.tensor_scalar(g_t1[:, :], g_i1[:, :], g_s1[:, :],
                                           None, ALU.mult)
                    C.produce("vch", i2)
                    C.wait(eng, "vch")
                    i2 = eng.scalar_tensor_tensor(gatetok[t][:, :], g_i2[:, :],
                                                  g_s2[:, :], g_t1[:, :], ALU.mult,
                                                  ALU.add)
                    C.produce("gtok", i2)
                else:
                    for _ in range(6):
                        C.produce("vch", None)
                    C.produce("gtok", None)

        def ffn_pass(xsrc, row0, dst, drow0, out_ch):
            ln_pass(xsrc, row0, "chunks", out_ch, lnT_dst=lnT)
            gating()
            acc_done = [[None, None] for _ in range(NT)]
            for e in range(NEX):
                par = e % 2
                gregs = (0, 1) if par == 0 else (2, 3)
                vregs = (2, 3) if par == 0 else (0, 1)
                s1, s2, s3 = (3 * e) % 2, (3 * e + 1) % 2, (3 * e + 2) % 2
                for (src, slw, wid) in ((w1_in, s1, FF), (w2_in, s2, FF)):
                    for (ch, val) in wfree_req[slw]:
                        C.wait(he["gpsimd"], ch, val)
                    dma("gpsimd",
                        wslot[slw].ap().rearrange("p (k f) -> p k f", f=FF),
                        src[e].rearrange("(k r) f -> r k f", r=128), f"wld{slw}")
                w1v = wslot[s1].ap().rearrange("p (k f) -> p k f", f=FF)
                w2v = wslot[s2].ap().rearrange("p (k f) -> p k f", f=FF)
                w3v = wslot[s3].ap().rearrange("p (j f) -> p j f", f=D)
                gb = [C.count(f"Rf{gregs[0]}"), C.count(f"Rf{gregs[1]}")]
                C.wait(he["tensor"], f"wld{s1}")
                for j in range(NJ):
                    r = gregs[j % 2]
                    if e == 0 and j < 2:
                        C.wait(he["tensor"], "lnTc")
                        C.wait(he["tensor"], "gtok")
                    C.wait(he["tensor"], f"Rf{r}")
                    for sl in range(2):
                        for k in range(NK):
                            last = (sl == 1 and k == NK - 1)
                            if en == "tensor":
                                i2 = nc.tensor.matmul(
                                    RV[r][:, sl * 512:(sl + 1) * 512],
                                    w1v[:, k, j * 128:(j + 1) * 128],
                                    lnT[k][:, sl * 512:(sl + 1) * 512],
                                    start=(k == 0), stop=(k == NK - 1))
                                if last:
                                    C.produce(f"Rg{r}", i2)
                            elif last:
                                C.produce(f"Rg{r}", None)
                    C.wait(he["scalar"], f"Rg{r}")
                    if en == "scalar":
                        i2 = eng.activation(s_h[j][:, :], RV[r], AF.Silu)
                        C.produce(f"Rf{r}", i2)
                    else:
                        C.produce(f"Rf{r}", None)
                wfree_req[s1] = [(f"Rf{gregs[1]}", C.count(f"Rf{gregs[1]}"))]
                for (ch, val) in wfree_req[s3]:
                    C.wait(he["gpsimd"], ch, val)
                dma("gpsimd",
                    wslot[s3].ap().rearrange("p (j f) -> p j f", f=D),
                    w3_in[e].rearrange("(j r) f -> r j f", r=128), f"wld{s3}")
                vb = [C.count(f"Rf{vregs[0]}"), C.count(f"Rf{vregs[1]}")]
                C.wait(he["tensor"], f"wld{s2}")
                for j in range(NJ):
                    r = vregs[j % 2]
                    C.wait(he["tensor"], f"Rf{r}")
                    for sl in range(2):
                        for k in range(NK):
                            last = (sl == 1 and k == NK - 1)
                            if en == "tensor":
                                i2 = nc.tensor.matmul(
                                    RV[r][:, sl * 512:(sl + 1) * 512],
                                    w2v[:, k, j * 128:(j + 1) * 128],
                                    lnT[k][:, sl * 512:(sl + 1) * 512],
                                    start=(k == 0), stop=(k == NK - 1))
                                if last:
                                    C.produce(f"Rg{r}", i2)
                            elif last:
                                C.produce(f"Rg{r}", None)
                    C.wait(he["vector"], f"Rg{r}")
                    C.wait(he["vector"], f"Rf{gregs[j % 2]}",
                           gb[j % 2] + j // 2 + 1)
                    if en == "vector":
                        i2 = eng.tensor_tensor(s_h[j][:, :], s_h[j][:, :], RV[r],
                                               ALU.mult)
                        C.produce(f"Rf{r}", i2)
                    else:
                        C.produce(f"Rf{r}", None)
                wfree_req[s2] = [(f"Rf{vregs[1]}", C.count(f"Rf{vregs[1]}"))]
                C.wait(he["tensor"], f"wld{s3}")
                C.wait(he["tensor"], f"Rf{vregs[0]}", vb[0] + 4)
                C.wait(he["tensor"], f"Rf{vregs[1]}", vb[1] + 4)
                for t in range(NT):
                    for foh in range(2):
                        r = gregs[(t * 2 + foh) % 2]
                        C.wait(he["tensor"], f"Rf{r}")
                        for sl in range(2):
                            for j in range(NJ):
                                last = (sl == 1 and j == NJ - 1)
                                if en == "tensor":
                                    i2 = nc.tensor.matmul(
                                        RV[r][:, sl * 512:(sl + 1) * 512],
                                        s_h[j][:, t * 128:(t + 1) * 128],
                                        w3v[:, j, foh * 1024 + sl * 512:
                                            foh * 1024 + (sl + 1) * 512],
                                        start=(j == 0), stop=(j == NJ - 1))
                                    if last:
                                        C.produce(f"Rg{r}", i2)
                                elif last:
                                    C.produce(f"Rg{r}", None)
                        C.wait(he["vector"], f"Rg{r}")
                        if en == "vector":
                            i2 = eng.scalar_tensor_tensor(
                                acc[t][:, foh * 1024:(foh + 1) * 1024], RV[r],
                                gatetok[t][:, e:e + 1],
                                acc[t][:, foh * 1024:(foh + 1) * 1024],
                                ALU.mult, ALU.add)
                            C.produce(f"Rf{r}", i2)
                        else:
                            C.produce(f"Rf{r}", None)
                        if e == NEX - 1:
                            acc_done[t][foh] = (r, C.count(f"Rf{r}"))
                rlast = gregs[(NT * 2 - 1) % 2]
                wfree_req[s3] = [(f"Rf{rlast}", C.count(f"Rf{rlast}"))]
            for t in range(NT):
                for foh in range(2):
                    r, v = acc_done[t][foh]
                    C.wait(he["sync"], f"Rf{r}", v)
                dma("sync", dst[drow0 + t * 128: drow0 + (t + 1) * 128, :],
                    acc[t][:, :], out_ch)

        def proj_sweep(w_src, dst_d, ntiles, wch):
            for (ch, val) in wfree_req[0] + wfree_req[1]:
                C.wait(he["gpsimd"], ch, val)
            dma("gpsimd", wsp.ap().rearrange("p (k f) -> p k f", f=D),
                w_src.ap().rearrange("(k r) f -> r k f", r=128), wch)
            wspv = wsp.ap().rearrange("p (k f) -> p k f", f=D)
            for t in range(ntiles):
                par = t % 2
                regs = (0, 1) if par == 0 else (2, 3)
                C.wait(he["sync"], f"Rf{regs[0]}")
                C.wait(he["sync"], f"Rf{regs[1]}")
                dma("sync", lnblk[par].ap().rearrange("p (k c) -> p k c", c=128),
                    lnb_d[t].rearrange("k r c -> r k c"), f"lnbl{par}")
                lv = lnblk[par].ap().rearrange("p (k c) -> p k c", c=128)
                C.wait(he["tensor"], wch)
                C.wait(he["tensor"], f"lnbl{par}")
                claim(regs)
                for sl in range(4):
                    for k in range(NK):
                        last = (sl == 3 and k == NK - 1)
                        if en == "tensor":
                            i2 = nc.tensor.matmul(
                                PBIG[par][:, sl * 512:(sl + 1) * 512],
                                lv[:, k, :],
                                wspv[:, k, sl * 512:(sl + 1) * 512],
                                start=(k == 0), stop=(k == NK - 1))
                            if last:
                                C.produce(f"Rg{regs[0]}", i2)
                        elif last:
                            C.produce(f"Rg{regs[0]}", None)
                C.wait(he["vector"], f"Rg{regs[0]}")
                C.wait(he["vector"], f"pjw{par}")
                if en == "vector":
                    i2 = eng.tensor_copy(b1st[par][:, 0:1024], RV[regs[0]])
                    C.produce(f"Rf{regs[0]}", i2)
                    i2 = eng.tensor_copy(b1st[par][:, 1024:2048], RV[regs[1]])
                    C.produce(f"Rf{regs[1]}", i2)
                else:
                    C.produce(f"Rf{regs[0]}", None)
                    C.produce(f"Rf{regs[1]}", None)
                C.wait(he["scalar"], f"Rf{regs[0]}")
                C.wait(he["scalar"], f"Rf{regs[1]}")
                dma("scalar", dst_d[t * 128:(t + 1) * 128, :], b1st[par][:, :],
                    f"pjw{par}")
            rel = []
            for pp in (0, 1):
                rr_ = (0, 1) if pp == 0 else (2, 3)
                rel.append((f"Rf{rr_[0]}", C.count(f"Rf{rr_[0]}")))
                rel.append((f"Rf{rr_[1]}", C.count(f"Rf{rr_[1]}")))
            wfree_req[0] = rel
            wfree_req[1] = rel

        def scores_epair(ep):
            qslot = ep % 2
            for rr_ in range(4):
                C.wait(he["sync"], f"Rf{rr_}")
            qv = qep[qslot].ap().rearrange("p (sc c) -> p sc c", c=256)
            dma("sync",
                qep[qslot].ap().rearrange("p (sc c) -> p sc c", c=256),
                q_d.ap()[:, ep * 256:(ep + 1) * 256]
                   .rearrange("(sc r) c -> r sc c", r=128),
                f"qepl{qslot}")
            C.wait(he["tensor"], f"qepl{qslot}")
            claim((0, 1, 2, 3))
            for sc in range(NSC):
                ks = sc % 4
                C.wait(he["sync"], f"krf{ks}")
                if sc < 4:
                    for rr_ in range(4):
                        C.wait(he["sync"], f"Rf{rr_}")
                dma("sync", kring[ks][:, :], k_d[sc * 128:(sc + 1) * 128, :],
                    f"krl{ks}")
                C.wait(he["tensor"], f"krl{ks}")
                for e01 in range(2):
                    for sl in range(4):
                        last_k = (e01 == 1 and sl == 3)
                        last_all = (sc == NSC - 1 and sl == 3)
                        if en == "tensor":
                            i2 = nc.tensor.matmul(
                                PBIG[e01][:, sl * 512:(sl + 1) * 512],
                                qv[:, sc, e01 * 128:(e01 + 1) * 128],
                                kring[ks][:, sl * 512:(sl + 1) * 512],
                                start=(sc == 0), stop=(sc == NSC - 1))
                            if last_k and not last_all:
                                C.produce(f"krf{ks}", i2)
                            if last_all:
                                C.produce(f"Rg{2 * e01}", i2)
                                if e01 == 1:
                                    C.produce(f"krf{ks}", eng.nop())
                        else:
                            if last_k and not last_all:
                                C.produce(f"krf{ks}", None)
                            if last_all:
                                C.produce(f"Rg{2 * e01}", None)
                                if e01 == 1:
                                    C.produce(f"krf{ks}", None)
            for e01 in range(2):
                et = ep * 2 + e01
                C.wait(he["vector"], f"Rg{2 * e01}")
                C.wait(he["vector"], f"spw{e01}")
                if en == "vector":
                    i2 = eng.tensor_copy(pstage[e01][:, 0:1024], RV[2 * e01])
                    C.produce(f"Rf{2 * e01}", i2)
                    i2 = eng.tensor_copy(pstage[e01][:, 1024:2048],
                                         RV[2 * e01 + 1])
                    C.produce(f"Rf{2 * e01 + 1}", i2)
                else:
                    C.produce(f"Rf{2 * e01}", None)
                    C.produce(f"Rf{2 * e01 + 1}", None)
                C.wait(he["scalar"], f"Rf{2 * e01}")
                C.wait(he["scalar"], f"Rf{2 * e01 + 1}")
                dma("scalar", Sp_d[et * 128:(et + 1) * 128, :],
                    pstage[e01][:, :], f"spw{e01}")

        def score_reduce():
            C.wait(he["gpsimd"], "spw0")
            C.wait(he["gpsimd"], "spw1")
            if en == "gpsimd":
                i2 = eng.collective_compute(
                    "AllReduce", ALU.add, replica_groups=RGROUPS,
                    ins=[Sp_d.ap().opt()], outs=[Sf_d.ap().opt()])
                C.produce("ARd", i2, 1)
            else:
                C.produce("ARd", None, 1)

        def softmax_pass():
            C.wait(he["sync"], "ARd", 1)
            for t in range(NK):
                par = t % 2
                C.wait(he["sync"], f"smexp{par}")
                dma("sync", pstage[par][:, :], Sf_d[t * 128:(t + 1) * 128, :],
                    f"sfl{par}")
                C.wait(he["vector"], f"sfl{par}")
                C.wait(he["vector"], f"smexp{par}")
                if en == "vector":
                    i2 = eng.reduce_max(smm[par][:, :], pstage[par][:, :], AX.X)
                    C.produce("vch", i2)
                    C.wait(eng, "vch")
                    i2 = eng.tensor_scalar(smb[par][:, :], smm[par][:, :],
                                           -0.125, None, ALU.mult)
                    C.produce(f"smb{par}", i2)
                else:
                    C.produce("vch", None)
                    C.produce(f"smb{par}", None)
                C.wait(he["scalar"], f"smb{par}")
                C.wait(he["scalar"], f"smsc{par}")
                C.wait(he["scalar"], f"smP{par}")
                if en == "scalar":
                    i2 = eng.activation(pexp[par][:, :], pstage[par][:, :],
                                        AF.Exp, bias=smb[par][:, :], scale=0.125,
                                        accum_out=sms[par][:, :])
                    C.produce(f"smexp{par}", i2)
                else:
                    C.produce(f"smexp{par}", None)
                C.wait(he["vector"], f"smexp{par}")
                if en == "vector":
                    i2 = eng.reciprocal(smr[par][:, :], sms[par][:, :])
                    C.produce(f"smsc{par}", i2)
                    C.wait(eng, f"smsc{par}")
                    i2 = eng.tensor_scalar(pexp[par][:, :], pexp[par][:, :],
                                           smr[par][:, :], None, ALU.mult)
                    C.produce(f"smn{par}", i2)
                else:
                    C.produce(f"smsc{par}", None)
                    C.produce(f"smn{par}", None)
                C.wait(he["scalar"], f"smn{par}")
                dma("scalar", P_d[t][:, :], pexp[par][:, :], f"smP{par}")

        def wtilde():
            dma("gpsimd", wob.ap().rearrange("p (k f) -> p k f", f=D),
                wo_in.ap().rearrange("(k r) f -> r k f", r=128), "wold")
            wov = wob.ap().rearrange("p (k f) -> p k f", f=D)
            for ft in range(NK):
                par = ft % 2
                regs = (0, 1) if par == 0 else (2, 3)
                C.wait(he["sync"], f"Rf{regs[0]}")
                C.wait(he["sync"], f"Rf{regs[1]}")
                for g in range(2):
                    dma("sync",
                        pblk[par].ap().rearrange("p (g e c) -> p g e c",
                                                 g=2, c=128)[:, g, :, :],
                        P_d.ap()[g * 8:(g + 1) * 8, :, ft * 128:(ft + 1) * 128]
                           .rearrange("e r c -> r e c"),
                        f"pbl{par}")
                pbv = pblk[par].ap().rearrange("p (e c) -> p e c", c=128)
                C.wait(he["tensor"], "wold")
                C.wait(he["tensor"], f"pbl{par}")
                claim(regs)
                for sl in range(4):
                    for e in range(NK):
                        last = (sl == 3 and e == NK - 1)
                        if en == "tensor":
                            i2 = nc.tensor.matmul(
                                PBIG[par][:, sl * 512:(sl + 1) * 512],
                                pbv[:, e, :],
                                wov[:, e, sl * 512:(sl + 1) * 512],
                                start=(e == 0), stop=(e == NK - 1))
                            if last:
                                C.produce(f"Rg{regs[0]}", i2)
                        elif last:
                            C.produce(f"Rg{regs[0]}", None)
                C.wait(he["vector"], f"Rg{regs[0]}")
                C.wait(he["vector"], f"wtw{par}")
                if en == "vector":
                    i2 = eng.tensor_copy(wtstage[par][:, 0:1024], RV[regs[0]])
                    C.produce(f"Rf{regs[0]}", i2)
                    i2 = eng.tensor_copy(wtstage[par][:, 1024:2048], RV[regs[1]])
                    C.produce(f"Rf{regs[1]}", i2)
                else:
                    C.produce(f"Rf{regs[0]}", None)
                    C.produce(f"Rf{regs[1]}", None)
                C.wait(he["scalar"], f"Rf{regs[0]}")
                C.wait(he["scalar"], f"Rf{regs[1]}")
                dma("scalar", Wt_d[ft][:, :], wtstage[par][:, :], f"wtw{par}")

        def av_apply():
            vtv = vTb.ap().rearrange("p (k s) -> p k s", s=D)
            for t in range(NK):
                par = t % 2
                C.wait(he["sync"], "vtbc")
                dma("sync", vtr[par][:, :], vt_d[t * 128:(t + 1) * 128, :],
                    f"vtrl{par}")
                C.wait(he["tensor"], f"vtrl{par}")
                for f in range(NK):
                    sl = f % 2
                    if t == 0 and f < 2:
                        C.wait(he["tensor"], f"Rf{sl}")
                    C.wait(he["tensor"], "vtbc", C.count("vtbc") - 1)
                    if en == "tensor":
                        i2 = nc.tensor.transpose(TSL[sl],
                                                 vtr[par][:, f * 128:(f + 1) * 128],
                                                 identb[:, :])
                        C.produce(f"Rg{sl}", i2)
                    else:
                        C.produce(f"Rg{sl}", None)
                    C.wait(he["vector"], f"Rg{sl}")
                    if en == "vector":
                        i2 = eng.tensor_copy(vtv[:, f, t * 128:(t + 1) * 128],
                                             TSL[sl])
                        C.produce("vtbc", i2)
                    else:
                        C.produce("vtbc", None)
            for f in range(NK):
                dma("sync", wtb[:, f * D:(f + 1) * D], Wt_d[f][:, :], "wtl")
            wtv = wtb.ap().rearrange("p (k f) -> p k f", f=D)
            for t in range(NK):
                par = t % 2
                regs = (0, 1) if par == 0 else (2, 3)
                C.wait(he["sync"], f"Rf{regs[0]}")
                C.wait(he["sync"], f"Rf{regs[1]}")
                dma("sync", x1r[par][:, :], X1_d[t * 128:(t + 1) * 128, :],
                    f"x1l{par}")
                C.wait(he["tensor"], "wtl")
                C.wait(he["tensor"], "vtbc")
                claim(regs)
                for sl in range(4):
                    for f in range(NK):
                        last = (sl == 3 and f == NK - 1)
                        if en == "tensor":
                            i2 = nc.tensor.matmul(
                                PBIG[par][:, sl * 512:(sl + 1) * 512],
                                vtv[:, f, t * 128:(t + 1) * 128],
                                wtv[:, f, sl * 512:(sl + 1) * 512],
                                start=(f == 0), stop=(f == NK - 1))
                            if last:
                                C.produce(f"Rg{regs[0]}", i2)
                        elif last:
                            C.produce(f"Rg{regs[0]}", None)
                C.wait(he["vector"], f"Rg{regs[0]}")
                C.wait(he["vector"], f"x1l{par}")
                C.wait(he["vector"], f"x2w{par}")
                if en == "vector":
                    i2 = eng.tensor_tensor(x2st[par][:, 0:1024],
                                           x1r[par][:, 0:1024], RV[regs[0]],
                                           ALU.add)
                    C.produce(f"Rf{regs[0]}", i2)
                    i2 = eng.tensor_tensor(x2st[par][:, 1024:2048],
                                           x1r[par][:, 1024:2048], RV[regs[1]],
                                           ALU.add)
                    C.produce(f"Rf{regs[1]}", i2)
                else:
                    C.produce(f"Rf{regs[0]}", None)
                    C.produce(f"Rf{regs[1]}", None)
                C.wait(he["scalar"], f"Rf{regs[0]}")
                C.wait(he["scalar"], f"Rf{regs[1]}")
                dma("scalar", X2_d[t * 128:(t + 1) * 128, :], x2st[par][:, :],
                    f"x2w{par}")

        if "A" in phases:
            for p in range(A_PASSES):
                ffn_pass(x_in, p * PT, X1_d, p * PT, "X1w")
            barrier(drains=("X1w",))

        if "B" in phases:
            for p in range(2):
                ln_pass(X1_d, p * PT, "blocks", "ntok", blk0=p * NT)
            barrier(drains=("lnbw0", "lnbw1"))

            proj_sweep(wq_in, q_d, NSC, "wldq")
            proj_sweep(wk_in, k_d, NSC, "wldk")
            barrier(drains=("pjw0", "pjw1"))

            for ep in range(8):
                scores_epair(ep)
            score_reduce()
            proj_sweep(wv_in, vt_d, NK, "wldv")
            softmax_pass()
            barrier(drains=("pjw0", "pjw1", "smP0", "smP1"))

            wtilde()
            barrier(drains=("wtw0", "wtw1"))

            av_apply()
            barrier(drains=("x2w0", "x2w1"))

        if "C" in phases:
            for p in range(C_PASSES):
                ffn_pass(X2_d, p * PT, y_out, p * PT, "yw")
            C.wait(he["sync"], "yw")

        dbg_srcs = dict(X1=X1_d, X2=X2_d, qd=q_d, kd=k_d, vt=vt_d, Pd=P_d,
                        Wt=Wt_d, lnb=lnb_d)
        if "st" in dbg_outs:
            C.wait(he["gpsimd"], "X1w")
            for i, t_ in enumerate((st_sum, st_sq1, st_sq2, st_mean, st_msq,
                                    st_var, st_rstd, eps_t)):
                dma("gpsimd", dbg_outs["st"][i], t_[:, :], "dbgw")
        if "acc0" in dbg_outs:
            C.wait(he["gpsimd"], "X1w")
            dma("gpsimd", dbg_outs["acc0"][:, :], acc[0][:, :], "dbgw")
        for nm, srcs in (("lnT", lnT), ("sh", s_h), ("gate", gatetok)):
            if nm in dbg_outs:
                C.wait(he["gpsimd"], "X1w")
                for i, t_ in enumerate(srcs):
                    dma("gpsimd", dbg_outs[nm][i], t_[:, :], "dbgw")
        if "lg" in dbg_outs:
            C.wait(he["gpsimd"], "X1w")
            dma("gpsimd", dbg_outs["lg"][:, :], lg_sb[:, :], "dbgw")
        if dbg_outs:
            for ch in ("X1w", "lnbw0", "lnbw1", "pjw0", "pjw1", "spw0", "spw1",
                       "smP0", "smP1", "wtw0", "wtw1", "x2w0", "x2w1", "yw"):
                C.wait(he["gpsimd"], ch)
        for name, outd in dbg_outs.items():
            if name not in dbg_srcs:
                continue
            src = dbg_srcs[name]
            if name in ("Pd", "Wt", "lnb"):
                dma("gpsimd", outd.ap(), src.ap(), "dbgw")
            else:
                dma("gpsimd", outd[:, :], src[:, :], "dbgw")
        C.wait(he["gpsimd"], "dbgw")
        C.wait(he["gpsimd"], "yw")

    with nc.Block() as block:
        @block.sync
        def _(e):
            emit("sync", e, Chans(sems))

        @block.scalar
        def _(e):
            emit("scalar", e, Chans(sems))

        @block.vector
        def _(e):
            emit("vector", e, Chans(sems))

        @block.tensor
        def _(e):
            emit("tensor", e, Chans(sems))

        @block.gpsimd
        def _(e):
            emit("gpsimd", e, Chans(sems))

    es.close()
    return nc


_NC_CACHE = {}


def kernel(L0_attn_x, L0_attn_Wq, L0_attn_Wk, L0_attn_Wv, L0_attn_Wo,
           L0_ffn_router_w, L0_ffn_w1, L0_ffn_w2, L0_ffn_w3):
    x = np.asarray(L0_attn_x, dtype=np.float32)
    Wq = np.ascontiguousarray(np.asarray(L0_attn_Wq, np.float32))
    Wk = np.ascontiguousarray(np.asarray(L0_attn_Wk, np.float32))
    Wv = np.ascontiguousarray(np.asarray(L0_attn_Wv, np.float32))
    Wo = np.ascontiguousarray(np.asarray(L0_attn_Wo, np.float32))
    rT = np.ascontiguousarray(np.asarray(L0_ffn_router_w, np.float32).T)
    w1 = np.ascontiguousarray(np.asarray(L0_ffn_w1, np.float32))
    w2 = np.ascontiguousarray(np.asarray(L0_ffn_w2, np.float32))
    w3 = np.ascontiguousarray(np.asarray(L0_ffn_w3, np.float32))
    ident = np.eye(128, dtype=np.float32)

    if "nc" not in _NC_CACHE:
        _NC_CACHE["nc"] = build()
    nc = _NC_CACHE["nc"]
    in_maps = []
    for c in range(8):
        b, h = c // 2, c % 2
        xp = np.ascontiguousarray(x[b, h * 2048:(h + 1) * 2048, :])
        in_maps.append({
            "x_in": xp, "wq_in": Wq, "wk_in": Wk, "wv_in": Wv, "wo_in": Wo,
            "rT_in": rT, "w1_in": w1, "w2_in": w2, "w3_in": w3, "id_in": ident,
        })
    res = run_bass_kernel_spmd(nc, in_maps, core_ids=list(range(8)))
    out = np.empty((4, 4096, 2048), dtype=np.float32)
    for c in range(8):
        b, h = c // 2, c % 2
        out[b, h * 2048:(h + 1) * 2048, :] = res.results[c]["y"]
    return out
